# version 04
# baseline (speedup 1.0000x reference)
import numpy as np
import ml_dtypes
from contextlib import ExitStack

import concourse.bass as bass
import concourse.mybir as mybir
from concourse.bass_utils import run_bass_kernel_spmd

F32 = mybir.dt.float32
BF16 = mybir.dt.bfloat16
AF = mybir.ActivationFunctionType
ALU = mybir.AluOpType
AX = mybir.AxisListType

B, T, C = 4, 2048, 2048
NH, NKV, D, E = 16, 4, 128, 8
GRP = NH // NKV
NT = T // 128
NO = NT // 2
NC = C // 128
EPS = 1e-5
SCALE = 1.0 / float(np.sqrt(D))
EXP_BIAS = -4.0
MASKNEG = -30000.0

CC_COS = 0
CC_SIN = CC_COS + NT * 64
CC_IDN = CC_SIN + NT * 64
CC_MSK4 = CC_IDN + 128
CC_GSC = CC_MSK4 + 512
CC_GBI = CC_GSC + 32
CC_IO8 = CC_GBI + 32
CC_W8 = CC_IO8 + 8
CC_N = CC_W8 + 8

VBLK = 4 * 130

ENGS = ("sy", "gp", "ve", "ac", "pe")


class Sem:
    def __init__(self, h):
        self.h = h
        self.n = 0

    def inc(self, k=1):
        self.n += k
        return self.n


class Plan:
    def __init__(self):
        self.ev = {e: [] for e in ENGS}

    def op(self, eng, fn, waits=(), inc=None, inc_n=1, extra_incs=()):
        ws = [(s.h, v) for (s, v) in waits if v > 0]
        ei = [(s.h, n) for (s, n) in extra_incs]
        self.ev[eng].append((fn, ws, inc.h if inc else None, inc_n, ei))
        for (s, n) in extra_incs:
            s.inc(n)
        if inc is not None:
            return inc.inc(inc_n)
        return None


def bcf(ap, reps):
    return bass.AP(ap.tensor, ap.offset, [ap.ap[0], [0, reps], ap.ap[1]])


def _rope_tables():
    import jax
    import jax.numpy as jnp
    cpu = jax.devices("cpu")[0]
    with jax.default_device(cpu):
        inv = 1.0 / 10000.0 ** (jnp.arange(0, 64, dtype=jnp.float32) / 64)
        pos = jnp.arange(0, T, dtype=jnp.float32)
        ang = pos[:, None] * inv[None, :]
        return np.asarray(jnp.cos(ang)), np.asarray(jnp.sin(ang))


_COS_T, _SIN_T = None, None


def build_consts(h):
    global _COS_T, _SIN_T
    if _COS_T is None:
        _COS_T, _SIN_T = _rope_tables()
    cst = np.zeros((128, CC_N), dtype=np.float32)
    for s in range(NT):
        gt = (s + 8 * h) % NT
        rows = slice(gt * 128, (gt + 1) * 128)
        cst[:, CC_COS + s * 64: CC_COS + (s + 1) * 64] = _COS_T[rows]
        cst[:, CC_SIN + s * 64: CC_SIN + (s + 1) * 64] = _SIN_T[rows]
    cst[:, CC_IDN: CC_IDN + 128] = np.eye(128, dtype=np.float32)
    pp = np.arange(128)[:, None]
    ff = np.arange(128)[None, :]
    cst[:, CC_MSK4: CC_MSK4 + 384] = np.float32(MASKNEG)
    cst[:, CC_MSK4 + 384: CC_MSK4 + 512] = np.where(pp > ff, np.float32(MASKNEG), np.float32(0.0))
    for qg in range(2):
        for kc in range(NT):
            gq_max = ((qg * 4 + 3) + 8 * h) % NT
            gk = (kc + 8 * h) % NT
            j = qg * NT + kc
            if gk > gq_max:
                cst[:, CC_GSC + j] = 0.0
                cst[:, CC_GBI + j] = MASKNEG
            else:
                cst[:, CC_GSC + j] = SCALE
                cst[:, CC_GBI + j] = EXP_BIAS
    cst[:, CC_IO8: CC_IO8 + 8] = np.arange(8, dtype=np.float32)[None, :]
    cst[:, CC_W8: CC_W8 + 8] = (8.0 - np.arange(8, dtype=np.float32))[None, :]
    return cst


def build_nc(max_phase=6, nt_ln=NT):
    import os as _os
    DBG = set(_os.environ.get("KDBG", "").split(","))
    nc = bass.Bass("TRN2", target_bir_lowering=False)
    x_rot = nc.dram_tensor("x_rot", [T, C], F32, kind="ExternalInput").ap()
    cst_h = nc.dram_tensor("cst", [128, CC_N], F32, kind="ExternalInput").ap()
    wr_h = nc.dram_tensor("wrt", [128, NC * 8], F32, kind="ExternalInput").ap()
    wq_h = nc.dram_tensor("wqt", [C, C], F32, kind="ExternalInput").ap()
    wk_h = nc.dram_tensor("wkt", [C, 512], F32, kind="ExternalInput").ap()
    wv_h = nc.dram_tensor("wvt", [C, 512], F32, kind="ExternalInput").ap()
    wp_h = nc.dram_tensor("wpt", [C, C], F32, kind="ExternalInput").ap()
    we_h = nc.dram_tensor("wet", [E, C, C], F32, kind="ExternalInput").ap()
    y_h = nc.dram_tensor("y", [T // 2, C], F32, kind="ExternalOutput").ap()

    es = ExitStack()
    sb = lambda nm, shp, dt=F32: es.enter_context(nc.sbuf_tensor(nm, shp, dt)).ap()
    ps = lambda nm, shp, dt=F32: es.enter_context(nc.psum_tensor(nm, shp, dt)).ap()

    cst = sb("cstt", [128, CC_N])
    wr = sb("wr", [128, NC * 8])
    idnb = sb("idnb", [128, 128], BF16)
    es_x = ExitStack()
    xr = [es_x.enter_context(nc.sbuf_tensor(f"xr{i}", [128, C], F32, side="right")).ap()
          for i in range(2)]
    xnT = sb("xnT", [128, NC * T], BF16)
    xf = [sb(f"xf{i}", [128, 128]) for i in range(2)]
    ktr = sb("ktr", [128, NKV * T], BF16)
    vsb = sb("vsb", [128, NT * VBLK], BF16)
    gbuf = sb("gbuf", [128, NO * C], BF16)
    wsl = [sb(f"wsl{i}", [128, NC * 512], BF16) for i in range(2)]
    t1b = sb("t1b", [128, 512])
    tmpc = sb("tmpc", [128, 512])
    rpc = [sb(f"rpc{i}", [128, 512], BF16) for i in range(2)]
    msks = sb("msks", [128, NO * 8])
    xres = [sb(f"xres{i}", [128, 512]) for i in range(2)]
    lg = sb("lg", [128, 8])
    sc1 = sb("sc1", [128, 1])
    sc2 = sb("sc2", [128, 1])
    sc3 = sb("sc3", [128, 1])
    sc4 = sb("sc4", [128, 1])
    sc5 = sb("sc5", [128, 1])
    rcp = [sb(f"rcp{i}", [128, 1]) for i in range(4)]

    pb = [ps(f"pb{i}", [128, 512]) for i in range(6)]
    ptp = [ps(f"ptp{i}", [128, 256], BF16) for i in range(2)]

    aoT = xnT

    s_x = [Sem(es.enter_context(nc.semaphore(f"s_x{i}"))) for i in range(2)]
    s_c = Sem(es.enter_context(nc.semaphore("s_c")))
    s_w = [Sem(es.enter_context(nc.semaphore(f"s_w{i}"))) for i in range(2)]
    s_cw = Sem(es.enter_context(nc.semaphore("s_cw")))
    s_ve = Sem(es.enter_context(nc.semaphore("s_ve")))
    s_ac = Sem(es.enter_context(nc.semaphore("s_ac")))
    s_pe = Sem(es.enter_context(nc.semaphore("s_pe")))
    s_st = [Sem(es.enter_context(nc.semaphore(f"s_st{i}"))) for i in range(2)]
    s_xr = [Sem(es.enter_context(nc.semaphore(f"s_xr{i}"))) for i in range(2)]
    s_cb = Sem(es.enter_context(nc.semaphore("s_cb")))

    P = Plan()

    P.op("sy", lambda e: e.dma_start(out=cst, in_=cst_h), inc=s_c, inc_n=16)
    P.op("sy", lambda e: e.dma_start(out=wr, in_=wr_h), inc=s_c, inc_n=16)
    P.op("gp", lambda e: e.dma_start(out=idnb, in_=cst_h[:, CC_IDN:CC_IDN + 128]), inc=s_cb, inc_n=16)

    wslices = [("k", 0)] + [("v", 0)] + [("q", f) for f in range(4)] + \
              [("e", (f, e)) for f in range(4) for e in range(E)] + \
              [("p", f) for f in range(4)]
    wld_done = {}
    cons_done = {}

    def emit_wload(k):
        kind, arg = wslices[k]
        slot = wsl[k % 2]
        if kind == "e":
            f, ex = arg
            base2d, c0 = we_h[ex], f * 512
        elif kind == "q":
            base2d, c0 = wq_h, arg * 512
        elif kind == "p":
            base2d, c0 = wp_h, arg * 512
        elif kind == "k":
            base2d, c0 = wk_h, 0
        else:
            base2d, c0 = wv_h, 0
        waits = [(s_cw, cons_done[k - 2])] if k >= 2 else []
        sw = s_w[k % 2]
        for p4 in range(4):
            def fn(e, base2d=base2d, c0=c0, p4=p4, slot=slot):
                src_ap = base2d[p4 * 512:(p4 + 1) * 512, c0:c0 + 512].rearrange(
                    "(c p) f -> p c f", c=4)
                return e.dma_start(out=slot[:, p4 * 4 * 512:(p4 + 1) * 4 * 512], in_=src_ap)
            P.op("gp", fn, waits=(waits if p4 == 0 else ()), inc=sw, inc_n=16)
        wld_done[k] = (sw, sw.n)

    emit_wload(0)
    emit_wload(1)
    n_loaded = 2

    def v_ones(e):
        ap = bass.AP(vsb.tensor, vsb.offset + 128,
                     [vsb.ap[0], [VBLK, NT], [130, 4], [1, 1]])
        return e.memset(ap, 1.0)
    vones_done = P.op("ve", v_ones, inc=s_ve)

    pg = [0]
    pg_free = {}

    def proj_mm(bank, slot, tok_tile, extra_waits, last_extra=()):
        for fi in range(NC):
            waits = list(extra_waits) if fi == 0 else []
            P.op("pe", lambda e, bank=bank, slot=slot, tok_tile=tok_tile, fi=fi: e.matmul(
                bank, xnT[:, fi * T + tok_tile * 128: fi * T + (tok_tile + 1) * 128],
                slot[:, fi * 512:(fi + 1) * 512],
                start=(fi == 0), stop=(fi == NC - 1)),
                waits=waits, inc=(s_pe if fi == NC - 1 else None))
        return s_pe.n

    def rope_piece(bank, sl, dst, waits):
        def fn(e):
            cos = cst[:, CC_COS + sl * 64: CC_COS + sl * 64 + 64]
            psin = cst[:, CC_SIN + sl * 64: CC_SIN + (sl + 1) * 64]
            pv8 = bass.AP(bank.tensor, bank.offset, [bank.ap[0], [64, 8], [1, 64]])
            t1v = bass.AP(t1b.tensor, t1b.offset, [t1b.ap[0], [64, 8], [1, 64]])
            e.tensor_tensor(t1v, pv8, bcf(cos, 8), ALU.mult)
            phi = bass.AP(bank.tensor, bank.offset + 64, [bank.ap[0], [128, 4], [1, 64]])
            plo = bass.AP(bank.tensor, bank.offset, [bank.ap[0], [128, 4], [1, 64]])
            t2lo = bass.AP(tmpc.tensor, tmpc.offset, [tmpc.ap[0], [128, 4], [1, 64]])
            t2hi = bass.AP(tmpc.tensor, tmpc.offset + 64, [tmpc.ap[0], [128, 4], [1, 64]])
            e.tensor_tensor(t2lo, phi, bcf(psin, 4), ALU.mult)
            e.tensor_tensor(t2hi, plo, bcf(psin, 4), ALU.mult)
            e.drain()
            dlo = bass.AP(dst.tensor, dst.offset, [dst.ap[0], [128, 4], [1, 64]])
            dhi = bass.AP(dst.tensor, dst.offset + 64, [dst.ap[0], [128, 4], [1, 64]])
            t1lo = bass.AP(t1b.tensor, t1b.offset, [t1b.ap[0], [128, 4], [1, 64]])
            t1hi = bass.AP(t1b.tensor, t1b.offset + 64, [t1b.ap[0], [128, 4], [1, 64]])
            e.tensor_tensor(dlo, t1lo, t2lo, ALU.subtract)
            i = e.tensor_tensor(dhi, t1hi, t2hi, ALU.add)
            e.drain()
            return i
        return P.op("ve", fn, waits=waits, inc=s_ve)

    def emit_kv(gt):
        banks, mmvv = [], []
        for which in range(2):
            bank = pb[pg[0] % 2]
            waits = [(s_ac, ln_done[gt]), wld_done[which]]
            if pg[0] >= 2:
                kind, val = pg_free[pg[0] - 2]
                waits.append(((s_ve if kind == "v" else s_ac), val))
            mmvv.append(proj_mm(bank, wsl[which], gt, waits))
            banks.append(bank)
            pg[0] += 1
        rv = rope_piece(banks[0], gt, rpc[gt % 2], [(s_pe, mmvv[0])])
        pg_free[pg[0] - 2] = ("v", rv)

        def vev(e, bank=banks[1], gt=gt):
            dst = bass.AP(vsb.tensor, vsb.offset + gt * VBLK,
                          [vsb.ap[0], [130, 4], [1, 128]])
            srcv = bass.AP(bank.tensor, bank.offset, [bank.ap[0], [128, 4], [1, 128]])
            return e.activation(out=dst, in_=srcv, func=AF.Identity)
        av = P.op("ac", vev, waits=[(s_pe, mmvv[1])], inc=s_ac)
        pg_free[pg[0] - 1] = ("a", av)
        for hh in range(4):
            P.op("pe", lambda e, hh=hh, i=gt % 2: e.transpose(
                ptp[hh % 2][:, 0:128], rpc[i][:, hh * 128:(hh + 1) * 128], idnb),
                waits=[(s_ve, rv), (s_ac, s_ac.n)], inc=s_pe)
            P.op("ac", lambda e, hh=hh, gt=gt: e.activation(
                out=ktr[:, hh * T + gt * 128: hh * T + (gt + 1) * 128],
                in_=ptp[hh % 2][:, 0:128], func=AF.Identity),
                waits=[(s_pe, s_pe.n)], inc=s_ac)

    ev_free = {}
    xf_free = {}
    tp_done_tile = {}
    ln_done = []
    rmask_done = []
    prev_xn = 0
    prev_b = 0
    prev_amax = 0

    for gt in range(nt_ln):
        waits = [(s_c, 32)]
        if gt >= 2:
            waits.append((s_pe, tp_done_tile[gt - 2]))
        P.op("sy", lambda e, gt=gt: e.dma_start(out=xr[gt % 2], in_=x_rot[gt * 128:(gt + 1) * 128, :]),
             waits=waits, inc=s_x[gt % 2], inc_n=16)
        sxv = s_x[gt % 2].n

        xa = xr[gt % 2]
        P.op("ve", lambda e, xa=xa: e.reduce_sum(sc1, xa, axis=AX.X), waits=[(s_x[gt % 2], sxv)])

        def negmean(e):
            e.drain()
            i = e.tensor_scalar(sc2, sc1, -1.0 / C, None, ALU.mult)
            e.drain()
            return i
        P.op("ve", negmean, inc=s_ve)

        if "nosq" not in DBG:
            a_sq = P.op("ac", lambda e, xa=xa: e.activation(
                out=gbuf[:, 0:C], in_=xa, func=AF.Square, accum_out=sc3[:, 0:1]),
                waits=[(s_x[gt % 2], sxv)], inc=s_ac)

        def bvec(e):
            e.tensor_tensor(sc4, sc2, sc2, ALU.mult)
            e.drain()
            i = e.tensor_scalar(sc4, sc4, -1.0, EPS, ALU.mult, ALU.add)
            e.drain()
            return i
        v_b = P.op("ve", bvec, waits=[(s_ac, prev_b)], inc=s_ve)

        def sqrtv(e):
            e.drain()
            return e.activation(out=sc5, in_=sc3, func=AF.Sqrt, bias=sc4[:, 0:1], scale=1.0 / C)
        a_sb = P.op("ac", sqrtv, waits=[(s_ve, v_b), (s_ve, prev_xn)], inc=s_ac)
        prev_b = a_sb

        def xn_apply(e, xa=xa):
            e.reciprocal(sc1, sc5[:, 0:1])
            e.drain()
            i = e.tensor_scalar(xa, xa, sc2[:, 0:1], sc1[:, 0:1], ALU.add, ALU.mult)
            e.drain()
            return i
        v_xn = P.op("ve", xn_apply, waits=[(s_ac, a_sb)], inc=s_ve)
        prev_xn = v_xn

        own = gt < NO and "norouter" not in DBG
        pend = []

        def emit_router(g, c, v_e, a_e):
            nonlocal prev_amax
            if own:
                rw = [(s_ve, v_e)]
                if c == 0:
                    rw.append((s_ve, prev_amax))
                P.op("pe", lambda e, g=g, c=c: e.matmul(
                    pb[5][:, 0:8], xf[g % 2], wr[:, c * 8:(c + 1) * 8],
                    start=(c == 0), stop=(c == NC - 1)),
                    waits=rw, inc=s_pe)
                xf_free[g] = [(s_pe, s_pe.n)] + ([] if "noacev" in DBG else [(s_ac, a_e)])
            else:
                xf_free[g] = [] if "noacev" in DBG else [(s_ac, a_e)]

        for c in range(NC if "notp" not in DBG else 0):
            g = gt * NC + c
            tb = pb[2 + (g % 2)]
            waits = [(s_ve, v_xn), (s_c, 32), (s_cb, 16)]
            if g >= 2:
                waits.append((s_ve, ev_free[g - 2]))
            P.op("pe", lambda e, tb=tb, xa=xa, c=c: e.transpose(
                tb[:, 0:128], xa[:, c * 128:(c + 1) * 128], cst[:, CC_IDN:CC_IDN + 128]),
                waits=waits, inc=s_pe)
            pe_tp = s_pe.n
            wv = [(s_pe, pe_tp)]
            if (g - 2) in xf_free:
                wv.extend(xf_free[g - 2])
            v_e = P.op("ve", lambda e, tb=tb, g=g: e.tensor_copy(xf[g % 2], tb[:, 0:128]),
                       waits=wv, inc=s_ve)
            ev_free[g] = v_e
            if "noacev" not in DBG:
                a_e = P.op("ac", lambda e, g=g, c=c, gt=gt: e.activation(
                    out=xnT[:, c * T + gt * 128: c * T + (gt + 1) * 128], in_=xf[g % 2],
                    func=AF.Identity), waits=[(s_ve, v_e)], inc=s_ac)
            else:
                a_e = s_ac.n
            pend.append((g, c, v_e, a_e))
            if len(pend) > 1:
                emit_router(*pend.pop(0))
        while pend:
            emit_router(*pend.pop(0))
        tp_done_tile[gt] = s_pe.n
        ln_done.append(s_ac.n)

        if own and "noamax" not in DBG:
            pe_r = s_pe.n

            def amax(e, gt=gt):
                e.tensor_copy(lg, pb[5][:, 0:8])
                e.drain()
                e.reduce_max(sc1, lg, axis=AX.X)
                e.drain()
                e.tensor_scalar(lg, lg, sc1[:, 0:1], None, ALU.is_equal)
                e.drain()
                e.tensor_mul(lg, lg, cst[:, CC_W8:CC_W8 + 8])
                e.drain()
                e.reduce_max(sc1, lg, axis=AX.X)
                e.drain()
                e.tensor_scalar(sc1, sc1, -1.0, 8.0, ALU.mult, ALU.add)
                e.drain()
                i = e.tensor_scalar(msks[:, gt * 8:(gt + 1) * 8],
                                     cst[:, CC_IO8:CC_IO8 + 8], sc1[:, 0:1], None, ALU.is_equal)
                e.drain()
                return i
            P.op("ve", amax, waits=[(s_pe, pe_r)], inc=s_ve)
            rmask_done.append(s_ve.n)
            prev_amax = s_ve.n
        if max_phase >= 2:
            emit_kv(gt)

    _dn = [0]

    def dummy_out():
        _dn[0] += 1
        dsem = Sem(es.enter_context(nc.semaphore(f"s_dummy{_dn[0]}")))
        for a in range(8):
            P.op("gp", lambda e, a=a: e.dma_start(out=y_h[a * 128:(a + 1) * 128, :],
                                                  in_=gbuf[:, a * C:(a + 1) * C]),
                 inc=dsem, inc_n=16)
        P.op("gp", lambda e, n=dsem.n: e.wait_ge(dsem.h, n))

    if max_phase < 2:
        dummy_out()

    pg = [0]
    pg_free = {}

    def proj_mm(bank, slot, tok_tile, extra_waits, last_extra=()):
        for fi in range(NC):
            waits = list(extra_waits) if fi == 0 else []
            P.op("pe", lambda e, bank=bank, slot=slot, tok_tile=tok_tile, fi=fi: e.matmul(
                bank, xnT[:, fi * T + tok_tile * 128: fi * T + (tok_tile + 1) * 128],
                slot[:, fi * 512:(fi + 1) * 512],
                start=(fi == 0), stop=(fi == NC - 1)),
                waits=waits, inc=(s_pe if fi == NC - 1 else None),
                extra_incs=(last_extra if fi == NC - 1 else ()))
        return s_pe.n

    def rope_piece(bank, sl, dst, waits):
        def fn(e):
            cos = cst[:, CC_COS + sl * 64: CC_COS + sl * 64 + 64]
            psin = cst[:, CC_SIN + sl * 64: CC_SIN + (sl + 1) * 64]
            pv8 = bass.AP(bank.tensor, bank.offset, [bank.ap[0], [64, 8], [1, 64]])
            t1v = bass.AP(t1b.tensor, t1b.offset, [t1b.ap[0], [64, 8], [1, 64]])
            e.tensor_tensor(t1v, pv8, bcf(cos, 8), ALU.mult)
            phi = bass.AP(bank.tensor, bank.offset + 64, [bank.ap[0], [128, 4], [1, 64]])
            plo = bass.AP(bank.tensor, bank.offset, [bank.ap[0], [128, 4], [1, 64]])
            t2lo = bass.AP(tmpc.tensor, tmpc.offset, [tmpc.ap[0], [128, 4], [1, 64]])
            t2hi = bass.AP(tmpc.tensor, tmpc.offset + 64, [tmpc.ap[0], [128, 4], [1, 64]])
            e.tensor_tensor(t2lo, phi, bcf(psin, 4), ALU.mult)
            e.tensor_tensor(t2hi, plo, bcf(psin, 4), ALU.mult)
            e.drain()
            dlo = bass.AP(dst.tensor, dst.offset, [dst.ap[0], [128, 4], [1, 64]])
            dhi = bass.AP(dst.tensor, dst.offset + 64, [dst.ap[0], [128, 4], [1, 64]])
            t1lo = bass.AP(t1b.tensor, t1b.offset, [t1b.ap[0], [128, 4], [1, 64]])
            t1hi = bass.AP(t1b.tensor, t1b.offset + 64, [t1b.ap[0], [128, 4], [1, 64]])
            e.tensor_tensor(dlo, t1lo, t2lo, ALU.subtract)
            i = e.tensor_tensor(dhi, t1hi, t2hi, ALU.add)
            e.drain()
            return i
        return P.op("ve", fn, waits=waits, inc=s_ve)

    def tp4_evac(src_ring_idx, dst_cols, rv):
        for hh in range(4):
            P.op("pe", lambda e, hh=hh, i=src_ring_idx: e.transpose(
                ptp[hh % 2][:, 0:128], rpc[i][:, hh * 128:(hh + 1) * 128], idnb),
                waits=[(s_ve, rv), (s_ac, s_ac.n)], inc=s_pe)
            P.op("ac", lambda e, hh=hh, dc=dst_cols[hh]: e.activation(
                out=xnT[:, dc:dc + 128] if dc >= 0 else None, in_=ptp[hh % 2][:, 0:128],
                func=AF.Identity), waits=[(s_pe, s_pe.n)], inc=s_ac)

    for gt in range(NT if max_phase >= 2 else 0):
        for which in range(2):
            bank = pb[pg[0] % 2]
            waits = [(s_ac, ln_done[gt]), wld_done[which]]
            if pg[0] >= 2:
                kind, val = pg_free[pg[0] - 2]
                waits.append(((s_ve if kind == "v" else s_ac), val))
            mmv = proj_mm(bank, wsl[which], gt, waits)
            if which == 0:
                rv = rope_piece(bank, gt, rpc[pg[0] % 2], [(s_pe, mmv)])
                pg_free[pg[0]] = ("v", rv)
                for hh in range(4):
                    P.op("pe", lambda e, hh=hh, i=pg[0] % 2: e.transpose(
                        ptp[hh % 2][:, 0:128], rpc[i][:, hh * 128:(hh + 1) * 128], idnb),
                        waits=[(s_ve, rv), (s_ac, s_ac.n)], inc=s_pe)
                    P.op("ac", lambda e, hh=hh, gt=gt: e.activation(
                        out=ktr[:, hh * T + gt * 128: hh * T + (gt + 1) * 128],
                        in_=ptp[hh % 2][:, 0:128], func=AF.Identity),
                        waits=[(s_pe, s_pe.n)], inc=s_ac)
            else:
                def vev(e, bank=bank, gt=gt):
                    dst = bass.AP(vsb.tensor, vsb.offset + gt * VBLK,
                                  [vsb.ap[0], [130, 4], [1, 128]])
                    srcv = bass.AP(bank.tensor, bank.offset, [bank.ap[0], [128, 4], [1, 128]])
                    return e.activation(out=dst, in_=srcv, func=AF.Identity)
                av = P.op("ac", vev, waits=[(s_pe, mmv)], inc=s_ac)
                pg_free[pg[0]] = ("a", av)
            pg[0] += 1
    P.op("pe", lambda e: e.nop(), waits=[(s_pe, s_pe.n)], inc=s_cw, inc_n=2)
    cons_done[0] = cons_done[1] = s_cw.n

    if max_phase < 3:
        dummy_out()
    if max_phase >= 3:
        emit_wload(2)
        emit_wload(3)
    n_loaded = 4

    qbar_ve, qbar_ac = s_ve.n, s_ac.n
    q_free = {}
    qg2 = 0
    qpend = []

    def q_consume(g, bank, ti, f, mmv):
        rv = rope_piece(bank, ti, rpc[g % 2], [(s_pe, mmv)])
        q_free[g] = rv
        for hh in range(4):
            head = f * 4 + hh
            P.op("pe", lambda e, hh=hh, i=g % 2: e.transpose(
                ptp[hh % 2][:, 0:128], rpc[i][:, hh * 128:(hh + 1) * 128], idnb),
                waits=[(s_ve, rv), (s_ac, s_ac.n)], inc=s_pe)
            P.op("ac", lambda e, head=head, ti=ti, hh=hh: e.activation(
                out=xnT[:, head * T + 1024 + ti * 128: head * T + 1024 + (ti + 1) * 128],
                in_=ptp[hh % 2][:, 0:128], func=AF.Identity),
                waits=[(s_pe, s_pe.n)], inc=s_ac)

    for f in range(4 if max_phase >= 3 else 0):
        for ti in range(NO):
            bank = pb[qg2 % 4]
            waits = [wld_done[2 + f]]
            if qg2 >= 4:
                waits.append((s_ve, q_free[qg2 - 4]))
            else:
                waits += [(s_ve, qbar_ve), (s_ac, qbar_ac)]
            mmv = proj_mm(bank, wsl[f % 2], ti, waits)
            qpend.append((qg2, bank, ti, f, mmv))
            if len(qpend) > 1:
                q_consume(*qpend.pop(0))
            qg2 += 1
        if f == 3:
            while qpend:
                q_consume(*qpend.pop(0))
        P.op("pe", lambda e: e.nop(), waits=[(s_pe, s_pe.n)], inc=s_cw)
        cons_done[2 + f] = s_cw.n
        if max_phase >= 4 or n_loaded <= 5:
            emit_wload(n_loaded)
        n_loaded += 1
    qtr_done = s_ac.n
    if max_phase < 4:
        dummy_out()

    moe_g = 0
    moe_free = {}
    for fog in range(4 if max_phase >= 4 else 0):
        for e_i in range(E):
            k = 6 + fog * 8 + e_i
            slot = wsl[k % 2]
            for ti in range(NO):
                bank = pb[moe_g % 4]
                waits = [wld_done[k], (s_ve, rmask_done[ti])]
                if moe_g >= 4:
                    waits.append((s_ve, moe_free[moe_g - 4]))
                else:
                    waits.append((s_ve, s_ve.n))
                for fi in range(NC):
                    P.op("pe", lambda e, bank=bank, slot=slot, ti=ti, fi=fi: e.matmul(
                        bank, xnT[:, fi * T + ti * 128: fi * T + (ti + 1) * 128],
                        slot[:, fi * 512:(fi + 1) * 512],
                        start=(fi == 0), stop=(fi == NC - 1)),
                        waits=(waits if fi == 0 else []), inc=(s_pe if fi == NC - 1 else None))
                mmv = s_pe.n
                gsl = gbuf[:, ti * C + fog * 512: ti * C + (fog + 1) * 512]
                if e_i == 0:
                    vv = P.op("ve", lambda e, bank=bank, gsl=gsl, ti=ti, e_i=e_i: e.tensor_scalar(
                        gsl, bank, msks[:, ti * 8 + e_i: ti * 8 + e_i + 1], None, ALU.mult),
                        waits=[(s_pe, mmv)], inc=s_ve)
                else:
                    vv = P.op("ve", lambda e, bank=bank, gsl=gsl, ti=ti, e_i=e_i: e.scalar_tensor_tensor(
                        gsl, bank, msks[:, ti * 8 + e_i: ti * 8 + e_i + 1], gsl, ALU.mult, ALU.add),
                        waits=[(s_pe, mmv)], inc=s_ve)
                moe_free[moe_g] = vv
                moe_g += 1
            P.op("pe", lambda e: e.nop(), waits=[(s_pe, s_pe.n)], inc=s_cw)
            cons_done[k] = s_cw.n
            if n_loaded < len(wslices):
                emit_wload(n_loaded)
                n_loaded += 1
        for ti in range(NO):
            gsl = gbuf[:, ti * C + fog * 512: ti * C + (fog + 1) * 512]
            P.op("ac", lambda e, gsl=gsl: e.activation(out=gsl, in_=gsl, func=AF.Silu),
                 waits=[(s_ve, s_ve.n)], inc=s_ac)
    while max_phase >= 4 and n_loaded < 40:
        emit_wload(n_loaded)
        n_loaded += 1
    silu_done = s_ac.n
    moe_ve_done = s_ve.n
    if max_phase < 5:
        dummy_out()

    es_x.close()
    aom = es.enter_context(nc.sbuf_tensor("aom", [128, 4 * C], BF16, side="right")).ap()
    atts = es.enter_context(nc.sbuf_tensor("atts", [128, 8 * 512], BF16, side="right")).ap()

    av_free = [("v", moe_ve_done)] * 4
    att_free = {kc: 0 for kc in range(8)}
    exp_done = {}
    aom_free_pe = 0
    for qg in range(2 if max_phase >= 5 else 0):
        for head in range(NH):
            kv = head // GRP
            avmm_done = {}

            def emit_s(kc):
                bank = pb[kc % 2]
                waits = [(s_ac, qtr_done)]
                if (head, qg, kc) in ((0, 0, 0), (0, 0, 1)):
                    waits.append((s_ve, moe_ve_done))
                prev_exp = exp_done.get((qg, head, kc - 2))
                if prev_exp is None and (head > 0 or qg > 0):
                    ph, pq = (head - 1, qg) if head > 0 else (NH - 1, qg - 1)
                    prev_exp = exp_done.get((pq, ph, kc - 2 + NT))
                if prev_exp is not None:
                    waits.append((s_ac, prev_exp))
                P.op("pe", lambda e, bank=bank, kv=kv, kc=kc, head=head, qg=qg: e.matmul(
                    bank, ktr[:, kv * T + kc * 128:kv * T + (kc + 1) * 128],
                    xnT[:, head * T + 1024 + qg * 512: head * T + 1024 + (qg + 1) * 512],
                    start=True, stop=True),
                    waits=waits, inc=s_pe)
                mmvs[kc] = s_pe.n

            mmvs = {}
            emit_s(0)
            emit_s(1)
            for kc in range(NT):
                bank = pb[kc % 2]
                mmv = mmvs[kc]
                vv = None
                if qg * 4 <= kc <= qg * 4 + 3:
                    qt = kc - qg * 4

                    def trif(e, bank=bank, qt=qt):
                        return e.tensor_add(bank[:, 0:(qt + 1) * 128], bank[:, 0:(qt + 1) * 128],
                                            cst[:, CC_MSK4 + (3 - qt) * 128: CC_MSK4 + 512])
                    vv = P.op("ve", trif, waits=[(s_pe, mmv)], inc=s_ve)
                j = qg * NT + kc
                ew = [(s_pe, mmv), (s_pe, att_free[kc % 8])]
                if vv:
                    ew.append((s_ve, vv))
                ae = P.op("ac", lambda e, bank=bank, kc=kc, j=j: e.activation(
                    out=atts[:, (kc % 8) * 512:(kc % 8 + 1) * 512], in_=bank, func=AF.Exp,
                    bias=cst[:, CC_GBI + j:CC_GBI + j + 1],
                    scale=cst[:, CC_GSC + j:CC_GSC + j + 1]),
                    waits=ew, inc=s_ac)
                exp_done[(qg, head, kc)] = ae
                if kc + 2 < NT:
                    emit_s(kc + 2)
                for qt in range(4):
                    wv = [(s_ac, ae)]
                    if kc == 0:
                        kind, val = av_free[qt]
                        wv.append(((s_ve if kind == "v" else s_ac), val))
                    P.op("pe", lambda e, qt=qt, kc=kc, kv=kv: e.matmul(
                        pb[2 + qt][:, 0:129], atts[:, (kc % 8) * 512 + qt * 128: (kc % 8) * 512 + (qt + 1) * 128],
                        vsb[:, kc * VBLK + kv * 130: kc * VBLK + kv * 130 + 129],
                        start=(kc == 0), stop=(kc == NT - 1)),
                        waits=wv, inc=(s_pe if qt == 3 else None))
                avmm_done[kc] = s_pe.n
                att_free[kc % 8] = s_pe.n
            for qt in range(4):
                def aoev(e, qt=qt, head=head):
                    e.reciprocal(rcp[qt], pb[2 + qt][:, 128:129])
                    e.drain()
                    return e.tensor_scalar(aom[:, qt * C + head * 128: qt * C + (head + 1) * 128],
                                           pb[2 + qt][:, 0:128], rcp[qt][:, 0:1], None, ALU.mult)
                wv = [(s_pe, avmm_done[NT - 1])]
                if head == 0:
                    wv.append((s_pe, aom_free_pe))
                vv = P.op("ve", aoev, waits=wv, inc=s_ve)
                av_free[qt] = ("v", vv)
        ao_done_v = s_ve.n
        for qt in range(4):
            ti = qg * 4 + qt
            for c in range(NC):
                P.op("pe", lambda e, qt=qt, c=c: e.transpose(
                    ptp[c % 2][:, 0:128], aom[:, qt * C + c * 128: qt * C + (c + 1) * 128], idnb),
                    waits=[(s_ve, ao_done_v), (s_ac, s_ac.n)], inc=s_pe)
                P.op("ac", lambda e, c=c, ti=ti: e.activation(
                    out=aoT[:, c * T + ti * 128: c * T + (ti + 1) * 128],
                    in_=ptp[c % 2][:, 0:128], func=AF.Identity),
                    waits=[(s_pe, s_pe.n)], inc=s_ac)
        aom_free_pe = s_pe.n
    aot_done = s_ac.n
    if max_phase < 6:
        dummy_out()

    wbar_ve = s_ve.n
    cmb = 0
    cmb_free = {}
    for f in range(4 if max_phase >= 6 else 0):
        k = 38 + f
        for ti in range(NO):
            xw = [(s_st[cmb % 2], (cmb // 2) * 16)]
            if cmb >= 2:
                xw.append((s_ve, cmb_free[cmb - 2]))
            P.op("sy", lambda e, ti=ti, f=f, i=cmb % 2: e.dma_start(
                out=xres[i], in_=x_rot[ti * 128:(ti + 1) * 128, f * 512:(f + 1) * 512]),
                waits=xw, inc=s_xr[cmb % 2], inc_n=16)
            xr_v = (s_xr[cmb % 2], s_xr[cmb % 2].n)
            bank = pb[cmb % 4]
            waits = [(s_ac, aot_done), wld_done[k]]
            if cmb >= 4:
                waits.append((s_ve, cmb_free[cmb - 4]))
            else:
                waits.append((s_ve, wbar_ve))
            if cmb >= 2:
                waits.append((s_ve, cmb_free[cmb - 2]))
            for fi in range(NC):
                P.op("pe", lambda e, bank=bank, k=k, ti=ti, fi=fi: e.matmul(
                    bank, aoT[:, fi * T + ti * 128: fi * T + (ti + 1) * 128],
                    wsl[k % 2][:, fi * 512:(fi + 1) * 512],
                    start=(fi == 0), stop=(fi == NC - 1)),
                    waits=(waits if fi == 0 else []), inc=(s_pe if fi == NC - 1 else None))
            mmv = s_pe.n

            def comb(e, bank=bank, ti=ti, f=f, i=cmb % 2):
                gsl = gbuf[:, ti * C + f * 512: ti * C + (f + 1) * 512]
                e.tensor_tensor(tmpc, bank, gsl, ALU.mult)
                e.drain()
                r = e.tensor_add(xres[i], tmpc, xres[i])
                e.drain()
                return r
            vv = P.op("ve", comb, waits=[(s_pe, mmv), xr_v, (s_ac, silu_done)], inc=s_ve)
            cmb_free[cmb] = vv
            P.op("sy", lambda e, ti=ti, f=f, i=cmb % 2: e.dma_start(
                out=y_h[ti * 128:(ti + 1) * 128, f * 512:(f + 1) * 512], in_=xres[i]),
                waits=[(s_ve, vv)], inc=s_st[cmb % 2], inc_n=16)
            cmb += 1
        P.op("pe", lambda e: e.nop(), waits=[(s_pe, s_pe.n)], inc=s_cw)
        cons_done[k] = s_cw.n
        if n_loaded < len(wslices):
            emit_wload(n_loaded)
            n_loaded += 1

    def finwait(e):
        e.wait_ge(s_st[0].h, s_st[0].n)
        return e.wait_ge(s_st[1].h, s_st[1].n)
    P.op("sy", finwait)

    with es:
        with nc.Block() as block:
            def runner(key):
                def go(eng):
                    for fn, waits, inch, incn, extra in P.ev[key]:
                        for sh, v in waits:
                            eng.wait_ge(sh, v)
                        ins = fn(eng)
                        if inch is not None:
                            ins.then_inc(inch, incn)
                        for sh, n in extra:
                            ins.then_inc(sh, n)
                return go
            block.sync(runner("sy"))
            block.gpsimd(runner("gp"))
            block.vector(runner("ve"))
            block.scalar(runner("ac"))
            block.tensor(runner("pe"))
    return nc


_NC_CACHE = None


def kernel(L0_attn_x, L0_attn_Wq, L0_attn_Wk, L0_attn_Wv, L0_attn_Wp,
           L0_ffn_w_router, L0_ffn_w_experts):
    global _NC_CACHE
    x = np.asarray(L0_attn_x, dtype=np.float32)
    wq = np.ascontiguousarray(np.asarray(L0_attn_Wq, dtype=np.float32).T)
    wk = np.ascontiguousarray(np.asarray(L0_attn_Wk, dtype=np.float32).T)
    wv = np.ascontiguousarray(np.asarray(L0_attn_Wv, dtype=np.float32).T)
    wp = np.ascontiguousarray(np.asarray(L0_attn_Wp, dtype=np.float32).T)
    wr = np.asarray(L0_ffn_w_router, dtype=np.float32)
    we = np.ascontiguousarray(np.asarray(L0_ffn_w_experts, dtype=np.float32).transpose(0, 2, 1))
    wrp = np.ascontiguousarray(wr.T.reshape(NC, 128, 8).transpose(1, 0, 2).reshape(128, NC * 8))

    if _NC_CACHE is None:
        _NC_CACHE = build_nc()
    nc = _NC_CACHE

    in_maps = []
    for core in range(8):
        b, h = core // 2, core % 2
        xb = x[b]
        x_rot = np.ascontiguousarray(np.concatenate(
            [xb[h * 1024:(h + 1) * 1024], xb[(1 - h) * 1024:(2 - h) * 1024]], axis=0))
        in_maps.append({
            "x_rot": x_rot,
            "cst": build_consts(h),
            "wrt": wrp,
            "wqt": wq, "wkt": wk, "wvt": wv, "wpt": wp, "wet": we,
        })
    res = run_bass_kernel_spmd(nc, in_maps, core_ids=list(range(8)))
    out = np.empty((B, T, C), dtype=np.float32)
    for core in range(8):
        b, h = core // 2, core % 2
        out[b, h * 1024:(h + 1) * 1024] = res.results[core]["y"]
    return out


if __name__ == "__main__":
    import jax
    import reference as R
    cpu = jax.devices("cpu")[0]
    with jax.default_device(cpu):
        inp = R.setup_inputs()
        inp = {k: np.asarray(v) for k, v in inp.items()}
        exp = np.asarray(R.reference(**inp))
    got = kernel(**inp)
    num = float(np.mean((got - exp) ** 2))
    den = float(np.mean(exp ** 2))
    print("resid var ratio:", num / den)


# version 05 (final)
# speedup vs baseline: 1.0267x; 1.0267x over previous
import numpy as np
import ml_dtypes
from contextlib import ExitStack

import concourse.bass as bass
import concourse.mybir as mybir
from concourse.bass_utils import run_bass_kernel_spmd

F32 = mybir.dt.float32
BF16 = mybir.dt.bfloat16
AF = mybir.ActivationFunctionType
ALU = mybir.AluOpType
AX = mybir.AxisListType

B, T, C = 4, 2048, 2048
NH, NKV, D, E = 16, 4, 128, 8
GRP = NH // NKV
NT = T // 128
NO = NT // 2
NC = C // 128
EPS = 1e-5
SCALE = 1.0 / float(np.sqrt(D))
EXP_BIAS = -4.0
MASKNEG = -30000.0
SC1 = 0.0883883461356163
SC2 = 1.512702185735293e-09

CC_COS = 0
CC_SIN = CC_COS + NT * 64
CC_IDN = CC_SIN + NT * 64
CC_MSK4 = CC_IDN + 128
CC_GSC = CC_MSK4 + 512
CC_GBI = CC_GSC + 32
CC_IO8 = CC_GBI + 32
CC_W8 = CC_IO8 + 8
CC_N = CC_W8 + 8

VBLK = 4 * 130

ENGS = ("sy", "gp", "ve", "ac", "pe")


class Sem:
    def __init__(self, h):
        self.h = h
        self.n = 0

    def inc(self, k=1):
        self.n += k
        return self.n


class Plan:
    def __init__(self):
        self.ev = {e: [] for e in ENGS}

    def op(self, eng, fn, waits=(), inc=None, inc_n=1, extra_incs=()):
        ws = [(s.h, v) for (s, v) in waits if v > 0]
        ei = [(s.h, n) for (s, n) in extra_incs]
        self.ev[eng].append((fn, ws, inc.h if inc else None, inc_n, ei))
        for (s, n) in extra_incs:
            s.inc(n)
        if inc is not None:
            return inc.inc(inc_n)
        return None


def bcf(ap, reps):
    return bass.AP(ap.tensor, ap.offset, [ap.ap[0], [0, reps], ap.ap[1]])


def _rope_tables():
    import jax
    import jax.numpy as jnp
    cpu = jax.devices("cpu")[0]
    with jax.default_device(cpu):
        inv = 1.0 / 10000.0 ** (jnp.arange(0, 64, dtype=jnp.float32) / 64)
        pos = jnp.arange(0, T, dtype=jnp.float32)
        ang = pos[:, None] * inv[None, :]
        return np.asarray(jnp.cos(ang)), np.asarray(jnp.sin(ang))


_COS_T, _SIN_T = None, None


def build_consts(h):
    global _COS_T, _SIN_T
    if _COS_T is None:
        _COS_T, _SIN_T = _rope_tables()
    cst = np.zeros((128, CC_N), dtype=np.float32)
    for s in range(NT):
        gt = (s + 8 * h) % NT
        rows = slice(gt * 128, (gt + 1) * 128)
        cst[:, CC_COS + s * 64: CC_COS + (s + 1) * 64] = _COS_T[rows]
        cst[:, CC_SIN + s * 64: CC_SIN + (s + 1) * 64] = _SIN_T[rows]
    cst[:, CC_IDN: CC_IDN + 128] = np.eye(128, dtype=np.float32)
    pp = np.arange(128)[:, None]
    ff = np.arange(128)[None, :]
    cst[:, CC_MSK4: CC_MSK4 + 384] = np.float32(MASKNEG)
    cst[:, CC_MSK4 + 384: CC_MSK4 + 512] = np.where(pp > ff, np.float32(MASKNEG), np.float32(0.0))
    for qg in range(2):
        for kc in range(NT):
            gq_max = ((qg * 4 + 3) + 8 * h) % NT
            gk = (kc + 8 * h) % NT
            j = qg * NT + kc
            if gk > gq_max:
                cst[:, CC_GSC + j] = 0.0
                cst[:, CC_GBI + j] = MASKNEG
            else:
                cst[:, CC_GSC + j] = 1.0
                cst[:, CC_GBI + j] = EXP_BIAS
    cst[:, CC_IO8: CC_IO8 + 8] = np.arange(8, dtype=np.float32)[None, :]
    cst[:, CC_W8: CC_W8 + 8] = (8.0 - np.arange(8, dtype=np.float32))[None, :]
    return cst


def build_nc(max_phase=6, nt_ln=NT):
    import os as _os
    DBG = set(_os.environ.get("KDBG", "").split(","))
    nc = bass.Bass("TRN2", target_bir_lowering=False)
    x_rot = nc.dram_tensor("x_rot", [T, C], F32, kind="ExternalInput").ap()
    cst_h = nc.dram_tensor("cst", [128, CC_N], F32, kind="ExternalInput").ap()
    wr_h = nc.dram_tensor("wrt", [128, NC * 8], F32, kind="ExternalInput").ap()
    wq_h = nc.dram_tensor("wqt", [C, C], F32, kind="ExternalInput").ap()
    wk_h = nc.dram_tensor("wkt", [C, 512], F32, kind="ExternalInput").ap()
    wv_h = nc.dram_tensor("wvt", [C, 512], F32, kind="ExternalInput").ap()
    wp_h = nc.dram_tensor("wpt", [C, C], F32, kind="ExternalInput").ap()
    we_h = nc.dram_tensor("wet", [E, C, C], F32, kind="ExternalInput").ap()
    y_h = nc.dram_tensor("y", [T // 2, C], F32, kind="ExternalOutput").ap()

    es = ExitStack()
    sb = lambda nm, shp, dt=F32: es.enter_context(nc.sbuf_tensor(nm, shp, dt)).ap()
    ps = lambda nm, shp, dt=F32: es.enter_context(nc.psum_tensor(nm, shp, dt)).ap()

    cst = sb("cstt", [128, CC_N])
    wr = sb("wr", [128, NC * 8])
    idnb = sb("idnb", [128, 128], BF16)
    es_x = ExitStack()
    xr = [es_x.enter_context(nc.sbuf_tensor(f"xr{i}", [128, C], F32, side="right")).ap()
          for i in range(2)]
    xnT = sb("xnT", [128, NC * T], BF16)
    xf = [sb(f"xf{i}", [128, 128]) for i in range(2)]
    ktr = sb("ktr", [128, NKV * T], BF16)
    vsb = sb("vsb", [128, NT * VBLK], BF16)
    gbuf = sb("gbuf", [128, NO * C], BF16)
    wsl = [sb(f"wsl{i}", [128, NC * 512], BF16) for i in range(2)]
    t1b = sb("t1b", [128, 512])
    tmpc = sb("tmpc", [128, 512])
    rpc = [sb(f"rpc{i}", [128, 512], BF16) for i in range(2)]
    msks = sb("msks", [128, NO * 8])
    xres = [sb(f"xres{i}", [128, 512]) for i in range(2)]
    lg = sb("lg", [128, 8])
    sc1 = sb("sc1", [128, 1])
    sc2 = sb("sc2", [128, 1])
    sc3 = sb("sc3", [128, 1])
    sc4 = sb("sc4", [128, 1])
    sc5 = sb("sc5", [128, 1])
    rcp = [sb(f"rcp{i}", [128, 1]) for i in range(4)]

    pb = [ps(f"pb{i}", [128, 512]) for i in range(6)]
    ptp = [ps(f"ptp{i}", [128, 256], BF16) for i in range(2)]

    aoT = xnT

    s_x = [Sem(es.enter_context(nc.semaphore(f"s_x{i}"))) for i in range(2)]
    s_c = Sem(es.enter_context(nc.semaphore("s_c")))
    s_w = [Sem(es.enter_context(nc.semaphore(f"s_w{i}"))) for i in range(2)]
    s_cw = Sem(es.enter_context(nc.semaphore("s_cw")))
    s_ve = Sem(es.enter_context(nc.semaphore("s_ve")))
    s_ac = Sem(es.enter_context(nc.semaphore("s_ac")))
    s_pe = Sem(es.enter_context(nc.semaphore("s_pe")))
    s_st = [Sem(es.enter_context(nc.semaphore(f"s_st{i}"))) for i in range(2)]
    s_xr = [Sem(es.enter_context(nc.semaphore(f"s_xr{i}"))) for i in range(2)]
    s_cb = Sem(es.enter_context(nc.semaphore("s_cb")))

    P = Plan()

    P.op("sy", lambda e: e.dma_start(out=cst, in_=cst_h), inc=s_c, inc_n=16)
    P.op("sy", lambda e: e.dma_start(out=wr, in_=wr_h), inc=s_c, inc_n=16)
    P.op("gp", lambda e: e.dma_start(out=idnb, in_=cst_h[:, CC_IDN:CC_IDN + 128]), inc=s_cb, inc_n=16)

    wslices = [("k", 0)] + [("v", 0)] + [("q", f) for f in range(4)] + \
              [("e", (f, e)) for f in range(4) for e in range(E)] + \
              [("p", f) for f in range(4)]
    wld_done = {}
    cons_done = {}

    def emit_wload(k):
        kind, arg = wslices[k]
        slot = wsl[k % 2]
        if kind == "e":
            f, ex = arg
            base2d, c0 = we_h[ex], f * 512
        elif kind == "q":
            base2d, c0 = wq_h, arg * 512
        elif kind == "p":
            base2d, c0 = wp_h, arg * 512
        elif kind == "k":
            base2d, c0 = wk_h, 0
        else:
            base2d, c0 = wv_h, 0
        waits = [(s_cw, cons_done[k - 2])] if k >= 2 else []
        sw = s_w[k % 2]
        for p4 in range(4):
            def fn(e, base2d=base2d, c0=c0, p4=p4, slot=slot):
                src_ap = base2d[p4 * 512:(p4 + 1) * 512, c0:c0 + 512].rearrange(
                    "(c p) f -> p c f", c=4)
                return e.dma_start(out=slot[:, p4 * 4 * 512:(p4 + 1) * 4 * 512], in_=src_ap)
            P.op("gp", fn, waits=(waits if p4 == 0 else ()), inc=sw, inc_n=16)
        wld_done[k] = (sw, sw.n)

    emit_wload(0)
    emit_wload(1)
    n_loaded = 2

    def v_ones(e):
        ap = bass.AP(vsb.tensor, vsb.offset + 128,
                     [vsb.ap[0], [VBLK, NT], [130, 4], [1, 1]])
        return e.memset(ap, 1.0)
    vones_done = P.op("ve", v_ones, inc=s_ve)

    pg = [0]
    pg_free = {}

    def proj_mm(bank, slot, tok_tile, extra_waits, last_extra=()):
        for fi in range(NC):
            waits = list(extra_waits) if fi == 0 else []
            P.op("pe", lambda e, bank=bank, slot=slot, tok_tile=tok_tile, fi=fi: e.matmul(
                bank, xnT[:, fi * T + tok_tile * 128: fi * T + (tok_tile + 1) * 128],
                slot[:, fi * 512:(fi + 1) * 512],
                start=(fi == 0), stop=(fi == NC - 1)),
                waits=waits, inc=(s_pe if fi == NC - 1 else None))
        return s_pe.n

    def rope_piece(bank, sl, dst, waits, do_scale=False):
        def fn(e):
            cos = cst[:, CC_COS + sl * 64: CC_COS + sl * 64 + 64]
            psin = cst[:, CC_SIN + sl * 64: CC_SIN + (sl + 1) * 64]
            pv8 = bass.AP(bank.tensor, bank.offset, [bank.ap[0], [64, 8], [1, 64]])
            t1v = bass.AP(t1b.tensor, t1b.offset, [t1b.ap[0], [64, 8], [1, 64]])
            e.tensor_tensor(t1v, pv8, bcf(cos, 8), ALU.mult)
            phi = bass.AP(bank.tensor, bank.offset + 64, [bank.ap[0], [128, 4], [1, 64]])
            plo = bass.AP(bank.tensor, bank.offset, [bank.ap[0], [128, 4], [1, 64]])
            t2lo = bass.AP(tmpc.tensor, tmpc.offset, [tmpc.ap[0], [128, 4], [1, 64]])
            t2hi = bass.AP(tmpc.tensor, tmpc.offset + 64, [tmpc.ap[0], [128, 4], [1, 64]])
            e.tensor_tensor(t2lo, phi, bcf(psin, 4), ALU.mult)
            e.tensor_tensor(t2hi, plo, bcf(psin, 4), ALU.mult)
            e.drain()
            t1lo = bass.AP(t1b.tensor, t1b.offset, [t1b.ap[0], [128, 4], [1, 64]])
            t1hi = bass.AP(t1b.tensor, t1b.offset + 64, [t1b.ap[0], [128, 4], [1, 64]])
            if not do_scale:
                dlo = bass.AP(dst.tensor, dst.offset, [dst.ap[0], [128, 4], [1, 64]])
                dhi = bass.AP(dst.tensor, dst.offset + 64, [dst.ap[0], [128, 4], [1, 64]])
                e.tensor_tensor(dlo, t1lo, t2lo, ALU.subtract)
                i = e.tensor_tensor(dhi, t1hi, t2hi, ALU.add)
                e.drain()
                return i
            e.tensor_tensor(t1lo, t1lo, t2lo, ALU.subtract)
            e.tensor_tensor(t1hi, t1hi, t2hi, ALU.add)
            e.drain()
            e.tensor_scalar(tmpc, t1b, SC2, None, ALU.mult)
            e.drain()
            i = e.scalar_tensor_tensor(dst, t1b, SC1, tmpc, ALU.mult, ALU.add)
            e.drain()
            return i
        return P.op("ve", fn, waits=waits, inc=s_ve)

    def emit_kv(gt):
        banks, mmvv = [], []
        for which in range(2):
            bank = pb[pg[0] % 2]
            waits = [(s_ac, ln_done[gt]), wld_done[which]]
            if pg[0] >= 2:
                kind, val = pg_free[pg[0] - 2]
                waits.append(((s_ve if kind == "v" else s_ac), val))
            mmvv.append(proj_mm(bank, wsl[which], gt, waits))
            banks.append(bank)
            pg[0] += 1
        rv = rope_piece(banks[0], gt, rpc[gt % 2], [(s_pe, mmvv[0])])
        pg_free[pg[0] - 2] = ("v", rv)

        def vev(e, bank=banks[1], gt=gt):
            dst = bass.AP(vsb.tensor, vsb.offset + gt * VBLK,
                          [vsb.ap[0], [130, 4], [1, 128]])
            srcv = bass.AP(bank.tensor, bank.offset, [bank.ap[0], [128, 4], [1, 128]])
            return e.activation(out=dst, in_=srcv, func=AF.Identity)
        av = P.op("ac", vev, waits=[(s_pe, mmvv[1])], inc=s_ac)
        pg_free[pg[0] - 1] = ("a", av)
        for hh in range(4):
            P.op("pe", lambda e, hh=hh, i=gt % 2: e.transpose(
                ptp[hh % 2][:, 0:128], rpc[i][:, hh * 128:(hh + 1) * 128], idnb),
                waits=[(s_ve, rv), (s_ac, s_ac.n)], inc=s_pe)
            P.op("ac", lambda e, hh=hh, gt=gt: e.activation(
                out=ktr[:, hh * T + gt * 128: hh * T + (gt + 1) * 128],
                in_=ptp[hh % 2][:, 0:128], func=AF.Identity),
                waits=[(s_pe, s_pe.n)], inc=s_ac)

    ev_free = {}
    xf_free = {}
    tp_done_tile = {}
    ln_done = []
    rmask_done = []
    prev_xn = 0
    prev_b = 0
    prev_amax = 0

    for gt in range(nt_ln):
        waits = [(s_c, 32)]
        if gt >= 2:
            waits.append((s_pe, tp_done_tile[gt - 2]))
        P.op("sy", lambda e, gt=gt: e.dma_start(out=xr[gt % 2], in_=x_rot[gt * 128:(gt + 1) * 128, :]),
             waits=waits, inc=s_x[gt % 2], inc_n=16)
        sxv = s_x[gt % 2].n

        xa = xr[gt % 2]
        P.op("ve", lambda e, xa=xa: e.reduce_sum(sc1, xa, axis=AX.X), waits=[(s_x[gt % 2], sxv)])

        def negmean(e):
            e.drain()
            i = e.tensor_scalar(sc2, sc1, -1.0 / C, None, ALU.mult)
            e.drain()
            return i
        P.op("ve", negmean, inc=s_ve)

        if "nosq" not in DBG:
            a_sq = P.op("ac", lambda e, xa=xa: e.activation(
                out=gbuf[:, 0:C], in_=xa, func=AF.Square, accum_out=sc3[:, 0:1]),
                waits=[(s_x[gt % 2], sxv)], inc=s_ac)

        def bvec(e):
            e.tensor_tensor(sc4, sc2, sc2, ALU.mult)
            e.drain()
            i = e.tensor_scalar(sc4, sc4, -1.0, EPS, ALU.mult, ALU.add)
            e.drain()
            return i
        v_b = P.op("ve", bvec, waits=[(s_ac, prev_b)], inc=s_ve)

        def sqrtv(e):
            e.drain()
            return e.activation(out=sc5, in_=sc3, func=AF.Sqrt, bias=sc4[:, 0:1], scale=1.0 / C)
        a_sb = P.op("ac", sqrtv, waits=[(s_ve, v_b), (s_ve, prev_xn)], inc=s_ac)
        prev_b = a_sb

        def xn_apply(e, xa=xa):
            e.reciprocal(sc1, sc5[:, 0:1])
            e.drain()
            i = e.tensor_scalar(xa, xa, sc2[:, 0:1], sc1[:, 0:1], ALU.add, ALU.mult)
            e.drain()
            return i
        v_xn = P.op("ve", xn_apply, waits=[(s_ac, a_sb)], inc=s_ve)
        prev_xn = v_xn

        own = gt < NO and "norouter" not in DBG
        pend = []

        def emit_router(g, c, v_e, a_e):
            nonlocal prev_amax
            if own:
                rw = [(s_ve, v_e)]
                if c == 0:
                    rw.append((s_ve, prev_amax))
                P.op("pe", lambda e, g=g, c=c: e.matmul(
                    pb[5][:, 0:8], xf[g % 2], wr[:, c * 8:(c + 1) * 8],
                    start=(c == 0), stop=(c == NC - 1)),
                    waits=rw, inc=s_pe)
                xf_free[g] = [(s_pe, s_pe.n)] + ([] if "noacev" in DBG else [(s_ac, a_e)])
            else:
                xf_free[g] = [] if "noacev" in DBG else [(s_ac, a_e)]

        for c in range(NC if "notp" not in DBG else 0):
            g = gt * NC + c
            tb = pb[2 + (g % 2)]
            waits = [(s_ve, v_xn), (s_c, 32), (s_cb, 16)]
            if g >= 2:
                waits.append((s_ve, ev_free[g - 2]))
            P.op("pe", lambda e, tb=tb, xa=xa, c=c: e.transpose(
                tb[:, 0:128], xa[:, c * 128:(c + 1) * 128], cst[:, CC_IDN:CC_IDN + 128]),
                waits=waits, inc=s_pe)
            pe_tp = s_pe.n
            wv = [(s_pe, pe_tp)]
            if (g - 2) in xf_free:
                wv.extend(xf_free[g - 2])
            v_e = P.op("ve", lambda e, tb=tb, g=g: e.tensor_copy(xf[g % 2], tb[:, 0:128]),
                       waits=wv, inc=s_ve)
            ev_free[g] = v_e
            if "noacev" not in DBG:
                a_e = P.op("ac", lambda e, g=g, c=c, gt=gt: e.activation(
                    out=xnT[:, c * T + gt * 128: c * T + (gt + 1) * 128], in_=xf[g % 2],
                    func=AF.Identity), waits=[(s_ve, v_e)], inc=s_ac)
            else:
                a_e = s_ac.n
            pend.append((g, c, v_e, a_e))
            if len(pend) > 1:
                emit_router(*pend.pop(0))
        while pend:
            emit_router(*pend.pop(0))
        tp_done_tile[gt] = s_pe.n
        ln_done.append(s_ac.n)

        if own and "noamax" not in DBG:
            pe_r = s_pe.n

            def amax(e, gt=gt):
                e.tensor_copy(lg, pb[5][:, 0:8])
                e.drain()
                e.reduce_max(sc1, lg, axis=AX.X)
                e.drain()
                e.tensor_scalar(lg, lg, sc1[:, 0:1], None, ALU.is_equal)
                e.drain()
                e.tensor_mul(lg, lg, cst[:, CC_W8:CC_W8 + 8])
                e.drain()
                e.reduce_max(sc1, lg, axis=AX.X)
                e.drain()
                e.tensor_scalar(sc1, sc1, -1.0, 8.0, ALU.mult, ALU.add)
                e.drain()
                i = e.tensor_scalar(msks[:, gt * 8:(gt + 1) * 8],
                                     cst[:, CC_IO8:CC_IO8 + 8], sc1[:, 0:1], None, ALU.is_equal)
                e.drain()
                return i
            P.op("ve", amax, waits=[(s_pe, pe_r)], inc=s_ve)
            rmask_done.append(s_ve.n)
            prev_amax = s_ve.n
        if max_phase >= 2:
            emit_kv(gt)

    _dn = [0]

    def dummy_out():
        _dn[0] += 1
        dsem = Sem(es.enter_context(nc.semaphore(f"s_dummy{_dn[0]}")))
        for a in range(8):
            P.op("gp", lambda e, a=a: e.dma_start(out=y_h[a * 128:(a + 1) * 128, :],
                                                  in_=gbuf[:, a * C:(a + 1) * C]),
                 inc=dsem, inc_n=16)
        P.op("gp", lambda e, n=dsem.n: e.wait_ge(dsem.h, n))

    if max_phase < 2:
        dummy_out()

    pg = [0]
    pg_free = {}

    def proj_mm(bank, slot, tok_tile, extra_waits, last_extra=()):
        for fi in range(NC):
            waits = list(extra_waits) if fi == 0 else []
            P.op("pe", lambda e, bank=bank, slot=slot, tok_tile=tok_tile, fi=fi: e.matmul(
                bank, xnT[:, fi * T + tok_tile * 128: fi * T + (tok_tile + 1) * 128],
                slot[:, fi * 512:(fi + 1) * 512],
                start=(fi == 0), stop=(fi == NC - 1)),
                waits=waits, inc=(s_pe if fi == NC - 1 else None),
                extra_incs=(last_extra if fi == NC - 1 else ()))
        return s_pe.n

    def rope_piece(bank, sl, dst, waits, do_scale=False):
        def fn(e):
            cos = cst[:, CC_COS + sl * 64: CC_COS + sl * 64 + 64]
            psin = cst[:, CC_SIN + sl * 64: CC_SIN + (sl + 1) * 64]
            pv8 = bass.AP(bank.tensor, bank.offset, [bank.ap[0], [64, 8], [1, 64]])
            t1v = bass.AP(t1b.tensor, t1b.offset, [t1b.ap[0], [64, 8], [1, 64]])
            e.tensor_tensor(t1v, pv8, bcf(cos, 8), ALU.mult)
            phi = bass.AP(bank.tensor, bank.offset + 64, [bank.ap[0], [128, 4], [1, 64]])
            plo = bass.AP(bank.tensor, bank.offset, [bank.ap[0], [128, 4], [1, 64]])
            t2lo = bass.AP(tmpc.tensor, tmpc.offset, [tmpc.ap[0], [128, 4], [1, 64]])
            t2hi = bass.AP(tmpc.tensor, tmpc.offset + 64, [tmpc.ap[0], [128, 4], [1, 64]])
            e.tensor_tensor(t2lo, phi, bcf(psin, 4), ALU.mult)
            e.tensor_tensor(t2hi, plo, bcf(psin, 4), ALU.mult)
            e.drain()
            t1lo = bass.AP(t1b.tensor, t1b.offset, [t1b.ap[0], [128, 4], [1, 64]])
            t1hi = bass.AP(t1b.tensor, t1b.offset + 64, [t1b.ap[0], [128, 4], [1, 64]])
            if not do_scale:
                dlo = bass.AP(dst.tensor, dst.offset, [dst.ap[0], [128, 4], [1, 64]])
                dhi = bass.AP(dst.tensor, dst.offset + 64, [dst.ap[0], [128, 4], [1, 64]])
                e.tensor_tensor(dlo, t1lo, t2lo, ALU.subtract)
                i = e.tensor_tensor(dhi, t1hi, t2hi, ALU.add)
                e.drain()
                return i
            e.tensor_tensor(t1lo, t1lo, t2lo, ALU.subtract)
            e.tensor_tensor(t1hi, t1hi, t2hi, ALU.add)
            e.drain()
            e.tensor_scalar(tmpc, t1b, SC2, None, ALU.mult)
            e.drain()
            i = e.scalar_tensor_tensor(dst, t1b, SC1, tmpc, ALU.mult, ALU.add)
            e.drain()
            return i
        return P.op("ve", fn, waits=waits, inc=s_ve)

    def tp4_evac(src_ring_idx, dst_cols, rv):
        for hh in range(4):
            P.op("pe", lambda e, hh=hh, i=src_ring_idx: e.transpose(
                ptp[hh % 2][:, 0:128], rpc[i][:, hh * 128:(hh + 1) * 128], idnb),
                waits=[(s_ve, rv), (s_ac, s_ac.n)], inc=s_pe)
            P.op("ac", lambda e, hh=hh, dc=dst_cols[hh]: e.activation(
                out=xnT[:, dc:dc + 128] if dc >= 0 else None, in_=ptp[hh % 2][:, 0:128],
                func=AF.Identity), waits=[(s_pe, s_pe.n)], inc=s_ac)

    for gt in range(NT if max_phase >= 2 else 0):
        for which in range(2):
            bank = pb[pg[0] % 2]
            waits = [(s_ac, ln_done[gt]), wld_done[which]]
            if pg[0] >= 2:
                kind, val = pg_free[pg[0] - 2]
                waits.append(((s_ve if kind == "v" else s_ac), val))
            mmv = proj_mm(bank, wsl[which], gt, waits)
            if which == 0:
                rv = rope_piece(bank, gt, rpc[pg[0] % 2], [(s_pe, mmv)])
                pg_free[pg[0]] = ("v", rv)
                for hh in range(4):
                    P.op("pe", lambda e, hh=hh, i=pg[0] % 2: e.transpose(
                        ptp[hh % 2][:, 0:128], rpc[i][:, hh * 128:(hh + 1) * 128], idnb),
                        waits=[(s_ve, rv), (s_ac, s_ac.n)], inc=s_pe)
                    P.op("ac", lambda e, hh=hh, gt=gt: e.activation(
                        out=ktr[:, hh * T + gt * 128: hh * T + (gt + 1) * 128],
                        in_=ptp[hh % 2][:, 0:128], func=AF.Identity),
                        waits=[(s_pe, s_pe.n)], inc=s_ac)
            else:
                def vev(e, bank=bank, gt=gt):
                    dst = bass.AP(vsb.tensor, vsb.offset + gt * VBLK,
                                  [vsb.ap[0], [130, 4], [1, 128]])
                    srcv = bass.AP(bank.tensor, bank.offset, [bank.ap[0], [128, 4], [1, 128]])
                    return e.activation(out=dst, in_=srcv, func=AF.Identity)
                av = P.op("ac", vev, waits=[(s_pe, mmv)], inc=s_ac)
                pg_free[pg[0]] = ("a", av)
            pg[0] += 1
    P.op("pe", lambda e: e.nop(), waits=[(s_pe, s_pe.n)], inc=s_cw, inc_n=2)
    cons_done[0] = cons_done[1] = s_cw.n

    if max_phase < 3:
        dummy_out()
    if max_phase >= 3:
        emit_wload(2)
        emit_wload(3)
    n_loaded = 4

    qbar_ve, qbar_ac = s_ve.n, s_ac.n
    q_free = {}
    qg2 = 0
    qpend = []

    def q_consume(g, bank, ti, f, mmv):
        rv = rope_piece(bank, ti, rpc[g % 2], [(s_pe, mmv)], do_scale=True)
        q_free[g] = rv
        for hh in range(4):
            head = f * 4 + hh
            P.op("pe", lambda e, hh=hh, i=g % 2: e.transpose(
                ptp[hh % 2][:, 0:128], rpc[i][:, hh * 128:(hh + 1) * 128], idnb),
                waits=[(s_ve, rv), (s_ac, s_ac.n)], inc=s_pe)
            P.op("ac", lambda e, head=head, ti=ti, hh=hh: e.activation(
                out=xnT[:, head * T + 1024 + ti * 128: head * T + 1024 + (ti + 1) * 128],
                in_=ptp[hh % 2][:, 0:128], func=AF.Identity),
                waits=[(s_pe, s_pe.n)], inc=s_ac)

    for f in range(4 if max_phase >= 3 else 0):
        for ti in range(NO):
            bank = pb[qg2 % 4]
            waits = [wld_done[2 + f]]
            if qg2 >= 4:
                waits.append((s_ve, q_free[qg2 - 4]))
            else:
                waits += [(s_ve, qbar_ve), (s_ac, qbar_ac)]
            mmv = proj_mm(bank, wsl[f % 2], ti, waits)
            qpend.append((qg2, bank, ti, f, mmv))
            if len(qpend) > 1:
                q_consume(*qpend.pop(0))
            qg2 += 1
        if f == 3:
            while qpend:
                q_consume(*qpend.pop(0))
        P.op("pe", lambda e: e.nop(), waits=[(s_pe, s_pe.n)], inc=s_cw)
        cons_done[2 + f] = s_cw.n
        if max_phase >= 4 or n_loaded <= 5:
            emit_wload(n_loaded)
        n_loaded += 1
    qtr_done = s_ac.n
    if max_phase < 4:
        dummy_out()

    moe_g = 0
    moe_free = {}
    for fog in range(4 if max_phase >= 4 else 0):
        for e_i in range(E):
            k = 6 + fog * 8 + e_i
            slot = wsl[k % 2]
            for ti in range(NO):
                bank = pb[moe_g % 4]
                waits = [wld_done[k], (s_ve, rmask_done[ti])]
                if moe_g >= 4:
                    waits.append((s_ve, moe_free[moe_g - 4]))
                else:
                    waits.append((s_ve, s_ve.n))
                for fi in range(NC):
                    P.op("pe", lambda e, bank=bank, slot=slot, ti=ti, fi=fi: e.matmul(
                        bank, xnT[:, fi * T + ti * 128: fi * T + (ti + 1) * 128],
                        slot[:, fi * 512:(fi + 1) * 512],
                        start=(fi == 0), stop=(fi == NC - 1)),
                        waits=(waits if fi == 0 else []), inc=(s_pe if fi == NC - 1 else None))
                mmv = s_pe.n
                gsl = gbuf[:, ti * C + fog * 512: ti * C + (fog + 1) * 512]
                if e_i == 0:
                    vv = P.op("ve", lambda e, bank=bank, gsl=gsl, ti=ti, e_i=e_i: e.tensor_scalar(
                        gsl, bank, msks[:, ti * 8 + e_i: ti * 8 + e_i + 1], None, ALU.mult),
                        waits=[(s_pe, mmv)], inc=s_ve)
                else:
                    vv = P.op("ve", lambda e, bank=bank, gsl=gsl, ti=ti, e_i=e_i: e.scalar_tensor_tensor(
                        gsl, bank, msks[:, ti * 8 + e_i: ti * 8 + e_i + 1], gsl, ALU.mult, ALU.add),
                        waits=[(s_pe, mmv)], inc=s_ve)
                moe_free[moe_g] = vv
                moe_g += 1
            P.op("pe", lambda e: e.nop(), waits=[(s_pe, s_pe.n)], inc=s_cw)
            cons_done[k] = s_cw.n
            if n_loaded < len(wslices):
                emit_wload(n_loaded)
                n_loaded += 1
        for ti in range(NO):
            gsl = gbuf[:, ti * C + fog * 512: ti * C + (fog + 1) * 512]
            P.op("ac", lambda e, gsl=gsl: e.activation(out=gsl, in_=gsl, func=AF.Silu),
                 waits=[(s_ve, s_ve.n)], inc=s_ac)
    while max_phase >= 4 and n_loaded < 40:
        emit_wload(n_loaded)
        n_loaded += 1
    silu_done = s_ac.n
    moe_ve_done = s_ve.n
    if max_phase < 5:
        dummy_out()

    es_x.close()
    aom = es.enter_context(nc.sbuf_tensor("aom", [128, 4 * C], BF16, side="right")).ap()
    atts = es.enter_context(nc.sbuf_tensor("atts", [128, 8 * 512], BF16, side="right")).ap()

    av_free = [("v", moe_ve_done)] * 4
    att_free = {kc: 0 for kc in range(8)}
    exp_done = {}
    aom_free_pe = 0
    for qg in range(2 if max_phase >= 5 else 0):
        for head in range(NH):
            kv = head // GRP
            avmm_done = {}

            def emit_s(kc):
                bank = pb[kc % 2]
                waits = [(s_ac, qtr_done)]
                if (head, qg, kc) in ((0, 0, 0), (0, 0, 1)):
                    waits.append((s_ve, moe_ve_done))
                prev_exp = exp_done.get((qg, head, kc - 2))
                if prev_exp is None and (head > 0 or qg > 0):
                    ph, pq = (head - 1, qg) if head > 0 else (NH - 1, qg - 1)
                    prev_exp = exp_done.get((pq, ph, kc - 2 + NT))
                if prev_exp is not None:
                    waits.append((s_ac, prev_exp))
                P.op("pe", lambda e, bank=bank, kv=kv, kc=kc, head=head, qg=qg: e.matmul(
                    bank, ktr[:, kv * T + kc * 128:kv * T + (kc + 1) * 128],
                    xnT[:, head * T + 1024 + qg * 512: head * T + 1024 + (qg + 1) * 512],
                    start=True, stop=True),
                    waits=waits, inc=s_pe)
                mmvs[kc] = s_pe.n

            mmvs = {}
            emit_s(0)
            emit_s(1)
            for kc in range(NT):
                bank = pb[kc % 2]
                mmv = mmvs[kc]
                vv = None
                if qg * 4 <= kc <= qg * 4 + 3:
                    qt = kc - qg * 4

                    def trif(e, bank=bank, qt=qt):
                        return e.tensor_add(bank[:, 0:(qt + 1) * 128], bank[:, 0:(qt + 1) * 128],
                                            cst[:, CC_MSK4 + (3 - qt) * 128: CC_MSK4 + 512])
                    vv = P.op("ve", trif, waits=[(s_pe, mmv)], inc=s_ve)
                j = qg * NT + kc
                ew = [(s_pe, mmv), (s_pe, att_free[kc % 8])]
                if vv:
                    ew.append((s_ve, vv))
                ae = P.op("ac", lambda e, bank=bank, kc=kc, j=j: e.activation(
                    out=atts[:, (kc % 8) * 512:(kc % 8 + 1) * 512], in_=bank, func=AF.Exp,
                    bias=cst[:, CC_GBI + j:CC_GBI + j + 1],
                    scale=cst[:, CC_GSC + j:CC_GSC + j + 1]),
                    waits=ew, inc=s_ac)
                exp_done[(qg, head, kc)] = ae
                if kc + 2 < NT:
                    emit_s(kc + 2)
                for qt in range(4):
                    wv = [(s_ac, ae)]
                    if kc == 0:
                        kind, val = av_free[qt]
                        wv.append(((s_ve if kind == "v" else s_ac), val))
                    P.op("pe", lambda e, qt=qt, kc=kc, kv=kv: e.matmul(
                        pb[2 + qt][:, 0:129], atts[:, (kc % 8) * 512 + qt * 128: (kc % 8) * 512 + (qt + 1) * 128],
                        vsb[:, kc * VBLK + kv * 130: kc * VBLK + kv * 130 + 129],
                        start=(kc == 0), stop=(kc == NT - 1)),
                        waits=wv, inc=(s_pe if qt == 3 else None))
                avmm_done[kc] = s_pe.n
                att_free[kc % 8] = s_pe.n
            for qt in range(4):
                def aoev(e, qt=qt, head=head):
                    e.reciprocal(rcp[qt], pb[2 + qt][:, 128:129])
                    e.drain()
                    return e.tensor_scalar(aom[:, qt * C + head * 128: qt * C + (head + 1) * 128],
                                           pb[2 + qt][:, 0:128], rcp[qt][:, 0:1], None, ALU.mult)
                wv = [(s_pe, avmm_done[NT - 1])]
                if head == 0:
                    wv.append((s_pe, aom_free_pe))
                vv = P.op("ve", aoev, waits=wv, inc=s_ve)
                av_free[qt] = ("v", vv)
        ao_done_v = s_ve.n
        for qt in range(4):
            ti = qg * 4 + qt
            for c in range(NC):
                P.op("pe", lambda e, qt=qt, c=c: e.transpose(
                    ptp[c % 2][:, 0:128], aom[:, qt * C + c * 128: qt * C + (c + 1) * 128], idnb),
                    waits=[(s_ve, ao_done_v), (s_ac, s_ac.n)], inc=s_pe)
                P.op("ac", lambda e, c=c, ti=ti: e.activation(
                    out=aoT[:, c * T + ti * 128: c * T + (ti + 1) * 128],
                    in_=ptp[c % 2][:, 0:128], func=AF.Identity),
                    waits=[(s_pe, s_pe.n)], inc=s_ac)
        aom_free_pe = s_pe.n
    aot_done = s_ac.n
    if max_phase < 6:
        dummy_out()

    wbar_ve = s_ve.n
    cmb = 0
    cmb_free = {}
    for f in range(4 if max_phase >= 6 else 0):
        k = 38 + f
        for ti in range(NO):
            xw = [(s_st[cmb % 2], (cmb // 2) * 16)]
            if cmb >= 2:
                xw.append((s_ve, cmb_free[cmb - 2]))
            P.op("sy", lambda e, ti=ti, f=f, i=cmb % 2: e.dma_start(
                out=xres[i], in_=x_rot[ti * 128:(ti + 1) * 128, f * 512:(f + 1) * 512]),
                waits=xw, inc=s_xr[cmb % 2], inc_n=16)
            xr_v = (s_xr[cmb % 2], s_xr[cmb % 2].n)
            bank = pb[cmb % 4]
            waits = [(s_ac, aot_done), wld_done[k]]
            if cmb >= 4:
                waits.append((s_ve, cmb_free[cmb - 4]))
            else:
                waits.append((s_ve, wbar_ve))
            if cmb >= 2:
                waits.append((s_ve, cmb_free[cmb - 2]))
            for fi in range(NC):
                P.op("pe", lambda e, bank=bank, k=k, ti=ti, fi=fi: e.matmul(
                    bank, aoT[:, fi * T + ti * 128: fi * T + (ti + 1) * 128],
                    wsl[k % 2][:, fi * 512:(fi + 1) * 512],
                    start=(fi == 0), stop=(fi == NC - 1)),
                    waits=(waits if fi == 0 else []), inc=(s_pe if fi == NC - 1 else None))
            mmv = s_pe.n

            def comb(e, bank=bank, ti=ti, f=f, i=cmb % 2):
                gsl = gbuf[:, ti * C + f * 512: ti * C + (f + 1) * 512]
                e.tensor_tensor(tmpc, bank, gsl, ALU.mult)
                e.drain()
                r = e.tensor_add(xres[i], tmpc, xres[i])
                e.drain()
                return r
            vv = P.op("ve", comb, waits=[(s_pe, mmv), xr_v, (s_ac, silu_done)], inc=s_ve)
            cmb_free[cmb] = vv
            P.op("sy", lambda e, ti=ti, f=f, i=cmb % 2: e.dma_start(
                out=y_h[ti * 128:(ti + 1) * 128, f * 512:(f + 1) * 512], in_=xres[i]),
                waits=[(s_ve, vv)], inc=s_st[cmb % 2], inc_n=16)
            cmb += 1
        P.op("pe", lambda e: e.nop(), waits=[(s_pe, s_pe.n)], inc=s_cw)
        cons_done[k] = s_cw.n
        if n_loaded < len(wslices):
            emit_wload(n_loaded)
            n_loaded += 1

    def finwait(e):
        e.wait_ge(s_st[0].h, s_st[0].n)
        return e.wait_ge(s_st[1].h, s_st[1].n)
    P.op("sy", finwait)

    with es:
        with nc.Block() as block:
            def runner(key):
                def go(eng):
                    for fn, waits, inch, incn, extra in P.ev[key]:
                        for sh, v in waits:
                            eng.wait_ge(sh, v)
                        ins = fn(eng)
                        if inch is not None:
                            ins.then_inc(inch, incn)
                        for sh, n in extra:
                            ins.then_inc(sh, n)
                return go
            block.sync(runner("sy"))
            block.gpsimd(runner("gp"))
            block.vector(runner("ve"))
            block.scalar(runner("ac"))
            block.tensor(runner("pe"))
    return nc


_NC_CACHE = None


def kernel(L0_attn_x, L0_attn_Wq, L0_attn_Wk, L0_attn_Wv, L0_attn_Wp,
           L0_ffn_w_router, L0_ffn_w_experts):
    global _NC_CACHE
    x = np.asarray(L0_attn_x, dtype=np.float32)
    wq = np.ascontiguousarray(np.asarray(L0_attn_Wq, dtype=np.float32).T)
    wk = np.ascontiguousarray(np.asarray(L0_attn_Wk, dtype=np.float32).T)
    wv = np.ascontiguousarray(np.asarray(L0_attn_Wv, dtype=np.float32).T)
    wp = np.ascontiguousarray(np.asarray(L0_attn_Wp, dtype=np.float32).T)
    wr = np.asarray(L0_ffn_w_router, dtype=np.float32)
    we = np.ascontiguousarray(np.asarray(L0_ffn_w_experts, dtype=np.float32).transpose(0, 2, 1))
    wrp = np.ascontiguousarray(wr.T.reshape(NC, 128, 8).transpose(1, 0, 2).reshape(128, NC * 8))

    if _NC_CACHE is None:
        _NC_CACHE = build_nc()
    nc = _NC_CACHE

    in_maps = []
    for core in range(8):
        b, h = core // 2, core % 2
        xb = x[b]
        x_rot = np.ascontiguousarray(np.concatenate(
            [xb[h * 1024:(h + 1) * 1024], xb[(1 - h) * 1024:(2 - h) * 1024]], axis=0))
        in_maps.append({
            "x_rot": x_rot,
            "cst": build_consts(h),
            "wrt": wrp,
            "wqt": wq, "wkt": wk, "wvt": wv, "wpt": wp, "wet": we,
        })
    res = run_bass_kernel_spmd(nc, in_maps, core_ids=list(range(8)))
    out = np.empty((B, T, C), dtype=np.float32)
    for core in range(8):
        b, h = core // 2, core % 2
        out[b, h * 1024:(h + 1) * 1024] = res.results[core]["y"]
    return out


if __name__ == "__main__":
    import jax
    import reference as R
    cpu = jax.devices("cpu")[0]
    with jax.default_device(cpu):
        inp = R.setup_inputs()
        inp = {k: np.asarray(v) for k, v in inp.items()}
        exp = np.asarray(R.reference(**inp))
    got = kernel(**inp)
    num = float(np.mean((got - exp) ** 2))
    den = float(np.mean(exp ** 2))
    print("resid var ratio:", num / den)
